# version 01
# baseline (speedup 1.0000x reference)
import sys
import numpy as np
import ml_dtypes

try:
    import concourse.bass as bass
    import concourse.mybir as mybir
    from concourse.bass_utils import run_bass_kernel_spmd
except ImportError:
    sys.path.insert(0, "/opt/trn_rl_repo")
    import concourse.bass as bass
    import concourse.mybir as mybir
    from concourse.bass_utils import run_bass_kernel_spmd

from contextlib import ExitStack

BF16 = mybir.dt.bfloat16
F32 = mybir.dt.float32
AF = mybir.ActivationFunctionType
ALU = mybir.AluOpType
AX = mybir.AxisListType
bfdt = ml_dtypes.bfloat16

D, DT, TOK, BL, NB, TKV, NKB = 2048, 16, 2048, 512, 4, 2560, 5
NH, HG, NGRP, DH, RS = 16, 4, 4, 128, 16
FI, NIT, NE, SI, NEXP = 1408, 11, 16, 2816, 18
INV_SQ_DH = 1.0 / float(np.sqrt(128.0))
LORA_SC = 1.0 / 16.0
EPS_LN, EPS_RMS, NEG_BIG = 1e-5, 1e-6, -1.0e30

PHASES = 6


def build_program(phases=PHASES, debug=False):
    nc = bass.Bass("TRN2", target_bir_lowering=False, debug=False)

    def dram_in(name, shape, dt):
        return nc.dram_tensor(name, list(shape), dt, kind="ExternalInput").ap()

    xe = dram_in("xe", (DT, 128, TKV), F32)
    masks_d = dram_in("masks", (5, 128, 640), BF16)
    cos_d = dram_in("costab", (128, TKV), BF16)
    sin_d = dram_in("sintab", (128, TKV), BF16)
    wqT_d = dram_in("wqT", (D, D), F32)
    wkT_d = dram_in("wkT", (D, D), F32)
    wvT_d = dram_in("wvT", (D, D), F32)
    woT_d = dram_in("woT", (D, D), F32)
    qkvAT_d = dram_in("qkvAT", (D, 96), F32)
    qBT_d = dram_in("qBT", (RS, D), F32)
    kBT_d = dram_in("kBT", (RS, D), F32)
    vBT_d = dram_in("vBT", (RS, D), F32)
    oAT_d = dram_in("oAT", (D, RS), F32)
    oBT_d = dram_in("oBT", (RS, D), F32)
    nwT_d = dram_in("nwT", (128, DT), F32)
    gwT_d = dram_in("gwT", (D, NE), F32)
    w1T_d = dram_in("w1T", (NE, D, FI), BF16)
    w3T_d = dram_in("w3T", (NE, D, FI), BF16)
    w2T_d = dram_in("w2T", (NE, FI, D), BF16)
    sw1T_d = dram_in("sw1T", (D, SI), BF16)
    sw3T_d = dram_in("sw3T", (D, SI), BF16)
    sw2T_d = dram_in("sw2T", (SI, D), BF16)
    idb_d = dram_in("idb", (128, 128), BF16)
    idf_d = dram_in("idf", (128, 128), F32)
    id16f_d = dram_in("id16f", (16, 16), F32)
    selm_d = dram_in("selm", (16, 16 * 128), BF16)
    onerb_d = dram_in("onerb", (1, 128), BF16)
    onerf_d = dram_in("onerf", (1, 128), F32)
    onecb_d = dram_in("onecb", (128, 1), BF16)
    onecf_d = dram_in("onecf", (128, 1), F32)
    Serow_d = dram_in("serow_in", (1, 16), F32)

    sk = "ExternalOutput" if debug else "Internal"
    xn_scr = nc.dram_tensor("xn_scr", [NKB, DT, 128, BL], BF16, kind=sk).ap()
    attn_scr = nc.dram_tensor("attn_scr", [NB, NH, 128, BL], BF16, kind=sk).ap()
    x1_scr = nc.dram_tensor("x1_scr", [NB, DT, 128, BL], F32, kind=sk).ap()
    row_scr = nc.dram_tensor("row_scr", [2, NKB, BL], F32, kind=sk).ap()
    yT = nc.dram_tensor("yT", [DT, 128, TOK], F32, kind="ExternalOutput").ap()

    st = ExitStack()
    ec = st.enter_context

    class Sem:
        def __init__(self, name):
            self.h = ec(nc.semaphore(name))
            self.n = 0

    LD = Sem("sLD")
    SC = Sem("sSC")
    PE = Sem("sPE")
    VE = Sem("sVE")
    AC = Sem("sAC")
    GP = Sem("sGP")
    RW = Sem("sRW")
    PH = Sem("sPH")

    block = ec(nc.Block())
    ENGS = ["sync", "tensor", "vector", "scalar", "gpsimd"]

    def step(eng, waits, build, inc=None, k=1):
        if inc is not None:
            inc.n += k
        tgt = inc.n if inc is not None else None
        w = [(s, v) for (s, v) in waits if v]

        def body(q, w=w, build=build, inc=inc, k=k):
            for s, v in w:
                q.wait_ge(s.h, v)
            last = build(q)
            if inc is not None:
                last.then_inc(inc.h, k)
        getattr(block, eng)(body)
        return tgt

    def barrier():
        base = PH.n
        for e in ENGS:
            step(e, [], lambda q: q.nop(), inc=PH)
        scv, ldv_ = SC.n, LD.n
        for e in ENGS:
            step(e, [(PH, base + len(ENGS)), (SC, scv), (LD, ldv_)],
                 lambda q: q.nop())

    idb = ec(nc.sbuf_tensor("idb_s", [128, 128], BF16))
    idf = ec(nc.sbuf_tensor("idf_s", [128, 128], F32))
    id16f = ec(nc.sbuf_tensor("id16f_s", [16, 16], F32))
    onerb = ec(nc.sbuf_tensor("onerb_s", [1, 128], BF16))
    onerf = ec(nc.sbuf_tensor("onerf_s", [1, 128], F32))
    onecb = ec(nc.sbuf_tensor("onecb_s", [128, 1], BF16))
    onecf = ec(nc.sbuf_tensor("onecf_s", [128, 1], F32))
    nwT = ec(nc.sbuf_tensor("nwT_s", [128, DT], F32))
    nwTq = ec(nc.sbuf_tensor("nwTq_s", [128, DT], F32))
    gw_s = ec(nc.sbuf_tensor("gw_s", [128, DT * 16], F32))
    Se_row = ec(nc.sbuf_tensor("Se_row", [1, 16], F32))
    Se_col = ec(nc.sbuf_tensor("Se_col", [16, 1], F32))
    wstage = ec(nc.sbuf_tensor("wstage", [128, DT * 128], F32))
    rows = ec(nc.sbuf_tensor("rows", [1, 8 * BL], F32))
    RM, RQ, RT, RV, RR, RC, RE, RT2 = [i * BL for i in range(8)]

    with nc.psum_tensor("psSe", [16, 512], F32) as psSe:
        def p0_loads(q):
            for sb, dr in [(idb, idb_d), (idf, idf_d), (id16f, id16f_d),
                           (onerb, onerb_d), (onerf, onerf_d), (onecb, onecb_d),
                           (onecf, onecf_d), (nwT, nwT_d), (Se_row, Serow_d)]:
                q.dma_start(out=sb[:], in_=dr[:]).then_inc(LD.h, 16)
            return q.dma_start(out=gw_s[:].rearrange("p (a e) -> p a e", e=16),
                               in_=gwT_d.rearrange("(a p) e -> p a e", p=128))
        LD.n += 16 * 9
        p0_ld = step("sync", [], p0_loads, inc=LD, k=16)

        ve_nwq = step("vector", [(LD, p0_ld)],
                      lambda q: q.tensor_scalar_mul(nwTq[:], nwT[:], INV_SQ_DH),
                      inc=VE)

        def p0_se(q):
            last = None
            for a in range(DT):
                last = q.matmul(psSe[0:16, 0:1], gw_s[:, a * 16:(a + 1) * 16],
                                onecf[:], start=(a == 0), stop=(a == DT - 1))
            return last
        p0_pe = step("tensor", [(LD, p0_ld)], p0_se, inc=PE)
        p0_sev = step("vector", [(PE, p0_pe)],
                      lambda q: q.tensor_copy(Se_col[:], psSe[0:16, 0:1]), inc=VE)

        def p0_serow(q):
            last = None
            for e in range(16):
                last = q.dma_start(out=Se_row[0:1, e:e + 1], in_=Se_col[e:e + 1, 0:1])
                last.then_inc(SC.h, 16)
            return last
        SC.n += 16 * 15
        p0_se_done = step("scalar", [(VE, p0_sev)], p0_serow, inc=SC, k=16)

    sc_xn = [0] * NKB
    sc_row = 0
    sc_attn_all = 0
    pa_scope = ExitStack()
    pec = pa_scope.enter_context
    xnb = [pec(nc.sbuf_tensor(f"xnb{i}", [128, DT * BL], BF16)) for i in range(2)]
    u_sb = pec(nc.sbuf_tensor("u_sb", [16, 3 * TKV], BF16))

    if phases >= 1:
        p1_scope = ExitStack()
        _aec = p1_scope.enter_context
        xblk0 = _aec(nc.sbuf_tensor("xblk0", [128, DT * BL], F32))
        xblk1 = _aec(nc.sbuf_tensor("xblk1", [128, DT * BL], F32))
        qkvA_b = _aec(nc.sbuf_tensor("qkvA_b", [128, DT * 96], BF16))
        xc0 = _aec(nc.sbuf_tensor("xcst0", [128, BL], BF16))
        xc1 = _aec(nc.sbuf_tensor("xcst1", [128, BL], BF16))
        xs0 = _aec(nc.sbuf_tensor("xsq0", [128, BL], BF16))
        xs1 = _aec(nc.sbuf_tensor("xsq1", [128, BL], BF16))
        tmp1 = _aec(nc.sbuf_tensor("tmp1", [128, BL], F32))
        psSx = _aec(nc.psum_tensor("psSx", [128, 512], F32))
        psSq = _aec(nc.psum_tensor("psSq", [128, 512], F32))
        psBcC = _aec(nc.psum_tensor("psBcC", [128, 512], F32))
        psBcE = _aec(nc.psum_tensor("psBcE", [128, 512], F32))
        psU = _aec(nc.psum_tensor("psU", [96, 512], F32))
        if True:

            ld_qa = step("sync", [(SC, p0_se_done)],
                         lambda q: q.dma_start(
                             out=wstage[:, 0:DT * 96].rearrange("p (a o) -> p a o", o=96),
                             in_=qkvAT_d.rearrange("(a p) o -> p a o", p=128)),
                         inc=LD, k=16)

            def qa_fold(q):
                last = None
                for a in range(DT):
                    last = q.tensor_scalar_mul(qkvA_b[:, a * 96:(a + 1) * 96],
                                               wstage[:, a * 96:(a + 1) * 96],
                                               nwT[:, a:a + 1])
                return last
            ve_qa = step("vector", [(LD, ld_qa)], qa_fold, inc=VE)

            xblks = [xblk0, xblk1]
            xcs = [xc0, xc1]
            xss = [xs0, xs1]
            ve_xn = [0] * NKB
            ve_u = [0] * NKB
            row_done = [0] * NKB
            sc_row_prev = 0

            for jj in range(NKB):
                xb = xblks[jj % 2]
                xn = xnb[jj % 2]

                ld_x = step("sync", [(VE, ve_xn[jj - 2] if jj >= 2 else 0)],
                            lambda q, jj=jj, xb=xb: q.dma_start(
                                out=xb[:].rearrange("p (a t) -> p a t", t=BL),
                                in_=xe[:, :, jj * BL:(jj + 1) * BL].rearrange("a p t -> p a t")),
                            inc=LD, k=16)

                pe_tile = {}
                for a in range(DT):
                    sl = a % 2
                    w_ = [(LD, ld_x)]
                    if a >= 2:
                        w_.append((PE, pe_tile[a - 2]))

                    def cast_b(q, xb=xb, a=a, sl=sl):
                        q.activation(xcs[sl][:], xb[:, a * BL:(a + 1) * BL], AF.Copy)
                        return q.activation(xss[sl][:], xb[:, a * BL:(a + 1) * BL], AF.Square)
                    ac_v = step("scalar", w_, cast_b, inc=AC)

                    def stat_b(q, a=a, sl=sl):
                        q.matmul(psSx[0:1, :], onecb[:], xcs[sl][:],
                                 start=(a == 0), stop=(a == DT - 1))
                        return q.matmul(psSq[0:1, :], onecb[:], xss[sl][:],
                                        start=(a == 0), stop=(a == DT - 1))
                    w2_ = [(AC, ac_v)]
                    if a == 0 and jj > 0:
                        w2_.append((RW, row_done[jj - 1]))
                    pe_tile[a] = step("tensor", w2_, stat_b, inc=PE)
                pe_stat = pe_tile[DT - 1]

                def rstep(eng, build, extra=None):
                    w_ = [(RW, RW.n)]
                    if extra:
                        w_.append(extra)
                    return step(eng, w_, build, inc=RW)

                w0_ = [(RW, RW.n), (PE, pe_stat), (SC, sc_row_prev)]
                step("vector", w0_, lambda q: q.tensor_scalar_mul(
                    rows[0:1, RM:RM + BL], psSx[0:1, :], 1.0 / D), inc=RW)
                rstep("vector", lambda q: q.tensor_scalar_mul(
                    rows[0:1, RQ:RQ + BL], psSq[0:1, :], 1.0 / D))
                rstep("vector", lambda q: q.tensor_mul(
                    rows[0:1, RT:RT + BL], rows[0:1, RM:RM + BL], rows[0:1, RM:RM + BL]))
                rstep("vector", lambda q: q.tensor_sub(
                    rows[0:1, RV:RV + BL], rows[0:1, RQ:RQ + BL], rows[0:1, RT:RT + BL]))
                rstep("vector", lambda q: q.tensor_scalar_add(
                    rows[0:1, RT:RT + BL], rows[0:1, RV:RV + BL], EPS_LN))
                rstep("scalar", lambda q: q.activation(
                    rows[0:1, RT:RT + BL], rows[0:1, RT:RT + BL], AF.Sqrt))
                rstep("vector", lambda q: q.reciprocal(
                    rows[0:1, RR:RR + BL], rows[0:1, RT:RT + BL]))
                rstep("vector", lambda q: q.tensor_mul(
                    rows[0:1, RT:RT + BL], rows[0:1, RV:RV + BL], rows[0:1, RR:RR + BL]))
                rstep("vector", lambda q: q.tensor_mul(
                    rows[0:1, RT:RT + BL], rows[0:1, RT:RT + BL], rows[0:1, RR:RR + BL]))
                rstep("vector", lambda q: q.tensor_scalar_add(
                    rows[0:1, RT:RT + BL], rows[0:1, RT:RT + BL], EPS_RMS))
                rstep("scalar", lambda q: q.activation(
                    rows[0:1, RT:RT + BL], rows[0:1, RT:RT + BL], AF.Sqrt))
                rstep("vector", lambda q: q.reciprocal(
                    rows[0:1, RT2:RT2 + BL], rows[0:1, RT:RT + BL]))
                rstep("vector", lambda q: q.tensor_mul(
                    rows[0:1, RC:RC + BL], rows[0:1, RR:RR + BL], rows[0:1, RT2:RT2 + BL]))
                rstep("vector", lambda q: q.tensor_mul(
                    rows[0:1, RE:RE + BL], rows[0:1, RM:RM + BL], rows[0:1, RC:RC + BL]))
                rstep("vector", lambda q: q.tensor_scalar_mul(
                    rows[0:1, RE:RE + BL], rows[0:1, RE:RE + BL], -1.0))
                rstep("vector", lambda q: q.tensor_scalar_add(
                    rows[0:1, RT:RT + BL], rows[0:1, RR:RR + BL], 1.0))
                rstep("vector", lambda q: q.tensor_mul(
                    rows[0:1, RT2:RT2 + BL], rows[0:1, RM:RM + BL], rows[0:1, RR:RR + BL]))
                row_done[jj] = rstep("vector", lambda q: q.tensor_scalar_mul(
                    rows[0:1, RT2:RT2 + BL], rows[0:1, RT2:RT2 + BL], -1.0))

                def rowdma(q, jj=jj):
                    q.dma_start(out=row_scr[0, jj].unsqueeze(0),
                                in_=rows[0:1, RT:RT + BL]).then_inc(SC.h, 16)
                    return q.dma_start(out=row_scr[1, jj].unsqueeze(0),
                                       in_=rows[0:1, RT2:RT2 + BL])
                SC.n += 16
                sc_row = step("scalar", [(RW, row_done[jj])], rowdma, inc=SC, k=16)
                sc_row_prev = sc_row

                def bc_b(q):
                    q.matmul(psBcC[:, :], onerf[:], rows[0:1, RC:RC + BL],
                             start=True, stop=True)
                    return q.matmul(psBcE[:, :], onerf[:], rows[0:1, RE:RE + BL],
                                    start=True, stop=True)
                pe_bc = step("tensor",
                             [(RW, row_done[jj]), (VE, ve_xn[jj - 1] if jj >= 1 else 0)],
                             bc_b, inc=PE)

                def xn_b(q, xb=xb, xn=xn):
                    last = None
                    for a in range(DT):
                        q.tensor_mul(tmp1[:], xb[:, a * BL:(a + 1) * BL], psBcC[:, :])
                        last = q.tensor_add(xn[:, a * BL:(a + 1) * BL], tmp1[:], psBcE[:, :])
                    return last
                ve_xn[jj] = step("vector",
                                 [(PE, pe_bc), (LD, ld_x),
                                  (SC, sc_xn[jj - 2] if jj >= 2 else 0)],
                                 xn_b, inc=VE)

                def u_b(q, xn=xn):
                    last = None
                    for a in range(DT):
                        last = q.matmul(psU[:, :], qkvA_b[:, a * 96:(a + 1) * 96],
                                        xn[:, a * BL:(a + 1) * BL],
                                        start=(a == 0), stop=(a == DT - 1))
                    return last
                pe_u = step("tensor",
                            [(VE, ve_xn[jj]), (VE, ve_qa),
                             (VE, ve_u[jj - 1] if jj >= 1 else 0)],
                            u_b, inc=PE)

                def u_e(q, jj=jj):
                    q.tensor_scalar_mul(u_sb[0:16, jj * BL:(jj + 1) * BL],
                                        psU[0:16, :], LORA_SC)
                    q.tensor_scalar_mul(u_sb[0:16, TKV + jj * BL:TKV + (jj + 1) * BL],
                                        psU[32:48, :], LORA_SC)
                    return q.tensor_scalar_mul(
                        u_sb[0:16, 2 * TKV + jj * BL:2 * TKV + (jj + 1) * BL],
                        psU[64:80, :], LORA_SC)
                ve_u[jj] = step("vector", [(PE, pe_u)], u_e, inc=VE)

                sc_xn[jj] = step("scalar", [(VE, ve_xn[jj])],
                                 lambda q, jj=jj, xn=xn: q.dma_start(
                                     out=xn_scr[jj].rearrange("a p t -> p a t"),
                                     in_=xn[:].rearrange("p (a t) -> p a t", t=BL)),
                                 inc=SC, k=16)

        p1_scope.close()
        barrier()

    if phases >= 2:
        att_scope = ExitStack()
        _aec = att_scope.enter_context
        K8 = _aec(nc.sbuf_tensor("K8", [128, HG * TKV], BF16))
        Vk8 = _aec(nc.sbuf_tensor("Vk8", [128, HG * TKV], BF16))
        wbuf = _aec(nc.sbuf_tensor("wbuf", [128, HG * D], BF16))
        qB_b = _aec(nc.sbuf_tensor("qB_b", [16, D], BF16))
        kB_b = _aec(nc.sbuf_tensor("kB_b", [16, D], BF16))
        vB_b = _aec(nc.sbuf_tensor("vB_b", [16, D], BF16))
        stB = _aec(nc.sbuf_tensor("stB", [16, D], F32))
        masks_s = _aec(nc.sbuf_tensor("masks_s", [128, 5 * 640], BF16))
        cosb = _aec(nc.sbuf_tensor("cosb", [128, 2 * BL], BF16))
        sinb = _aec(nc.sbuf_tensor("sinb", [128, 2 * BL], BF16))
        kvb0 = _aec(nc.sbuf_tensor("kvb0", [128, BL], BF16))
        kvb1 = _aec(nc.sbuf_tensor("kvb1", [128, BL], BF16))
        rotb = _aec(nc.sbuf_tensor("rotb", [128, BL], BF16))
        ro1 = _aec(nc.sbuf_tensor("ro1", [128, BL], BF16))
        Qh0 = _aec(nc.sbuf_tensor("Qh0", [128, BL], BF16))
        Qh1 = _aec(nc.sbuf_tensor("Qh1", [128, BL], BF16))
        Pb0 = _aec(nc.sbuf_tensor("Pb0", [128, 640], BF16))
        Pb1 = _aec(nc.sbuf_tensor("Pb1", [128, 640], BF16))
        PT0 = _aec(nc.sbuf_tensor("PT0", [128, 640], BF16))
        PT1 = _aec(nc.sbuf_tensor("PT1", [128, 640], BF16))
        dt0 = _aec(nc.sbuf_tensor("dt0", [128, 128], BF16))
        dt1 = _aec(nc.sbuf_tensor("dt1", [128, 128], BF16))
        ath0 = _aec(nc.sbuf_tensor("ath0", [128, BL], BF16))
        ath1 = _aec(nc.sbuf_tensor("ath1", [128, BL], BF16))
        statm = _aec(nc.sbuf_tensor("statm", [128, 16], F32))
        psKV0 = _aec(nc.psum_tensor("psKV0", [128, 512], F32))
        psKV1 = _aec(nc.psum_tensor("psKV1", [128, 512], F32))
        if True:

            psKV = [psKV0, psKV1]
            kvb = [kvb0, kvb1]
            Qh = [Qh0, Qh1]
            Pb = [Pb0, Pb1]
            PTs = [PT0, PT1]
            dts = [dt0, dt1]
            ath = [ath0, ath1]

            ld_m = step("sync", [], lambda q: q.dma_start(
                out=masks_s[:].rearrange("p (m c) -> p m c", c=640),
                in_=masks_d.rearrange("m p c -> p m c")), inc=LD, k=16)
            for sb, dr, scale in [(qB_b, qBT_d, INV_SQ_DH), (kB_b, kBT_d, None),
                                  (vB_b, vBT_d, None)]:
                ldb = step("sync", [(VE, VE.n)],
                           lambda q, dr=dr: q.dma_start(out=stB[:], in_=dr[:]),
                           inc=LD, k=16)
                step("vector", [(LD, ldb)],
                     (lambda q, sb=sb, scale=scale:
                      q.tensor_scalar_mul(sb[:], stB[:], scale) if scale
                      else q.tensor_copy(sb[:], stB[:])),
                     inc=VE)
            ve_lorab = VE.n

            def emit_rope(q, srcb, cslot, dst_ap):
                q.tensor_copy(rotb[0:64, :], srcb[64:128, :])
                q.tensor_copy(rotb[64:128, :], srcb[0:64, :])
                q.tensor_mul(ro1[:], srcb[:], cosb[:, cslot * BL:(cslot + 1) * BL])
                q.tensor_mul(rotb[:], rotb[:], sinb[:, cslot * BL:(cslot + 1) * BL])
                return q.tensor_add(dst_ap, ro1[:], rotb[:])

            sc_cos = {}

            def load_costab(jj, prev_rope_ve):
                def b(q, jj=jj):
                    sl = jj % 2
                    q.dma_start(out=cosb[:, sl * BL:(sl + 1) * BL],
                                in_=cos_d[:, jj * BL:(jj + 1) * BL]).then_inc(SC.h, 16)
                    return q.dma_start(out=sinb[:, sl * BL:(sl + 1) * BL],
                                       in_=sin_d[:, jj * BL:(jj + 1) * BL])
                SC.n += 16
                return step("scalar", [(VE, prev_rope_ve)], b, inc=SC, k=16)

            pe_wbuf_free = 0
            pe_k8_free = 0

            def stream_strips(wT_src, fold_scale, grp):
                folds = []
                for h in range(HG):
                    hh = grp * HG + h
                    ldv = step("sync",
                               [(VE, folds[-1] if folds else 0),
                                (PE, pe_wbuf_free if h == 0 else 0)],
                               lambda q, hh=hh, wT_src=wT_src: q.dma_start(
                                   out=wstage[:].rearrange("p (a o) -> p a o", o=128),
                                   in_=wT_src[:, hh * 128:(hh + 1) * 128].rearrange(
                                       "(a p) o -> p a o", p=128)),
                               inc=LD, k=16)

                    def fold(q, h=h, fold_scale=fold_scale):
                        last = None
                        for a in range(DT):
                            last = q.tensor_scalar_mul(
                                wbuf[:, h * D + a * 128:h * D + (a + 1) * 128],
                                wstage[:, a * 128:(a + 1) * 128],
                                fold_scale[:, a:a + 1])
                        return last
                    folds.append(step("vector",
                                      [(LD, ldv), (PE, pe_wbuf_free if h == 0 else 0)],
                                      fold, inc=VE))
                return folds

            rope_prev = 0
            for grp in range(NGRP):
                folds = stream_strips(wkT_d, nwT, grp)
                pe_mm = {}
                ac_ev = {}
                ve_rp = {}
                ld_xn = {}
                pe_prev_part = PE.n
                ac_prev_part = AC.n
                for jj in range(NKB):
                    ld_xn[jj] = step("sync",
                                     [(SC, sc_xn[jj]),
                                      (PE, pe_mm.get((jj - 2) * HG + HG - 1,
                                                     pe_prev_part if jj < 2 else 0))],
                                     lambda q, jj=jj: q.dma_start(
                                         out=xnb[jj % 2][:].rearrange("p (a t) -> p a t", t=BL),
                                         in_=xn_scr[jj].rearrange("a p t -> p a t")),
                                     inc=LD, k=16)
                    cs = load_costab(jj, rope_prev)
                    for h in range(HG):
                        hh = grp * HG + h
                        i = jj * HG + h
                        sl = i % 2

                        def kmm(q, jj=jj, h=h, hh=hh, sl=sl):
                            for a in range(DT):
                                q.matmul(psKV[sl][:, :],
                                         wbuf[:, h * D + a * 128:h * D + (a + 1) * 128],
                                         xnb[jj % 2][:, a * BL:(a + 1) * BL],
                                         start=(a == 0), stop=False)
                            return q.matmul(psKV[sl][:, :],
                                            kB_b[0:16, hh * 128:(hh + 1) * 128],
                                            u_sb[0:16, TKV + jj * BL:TKV + (jj + 1) * BL],
                                            start=False, stop=True)
                        pe_mm[i] = step("tensor",
                                        [(LD, ld_xn[jj]), (VE, folds[h]), (VE, ve_lorab),
                                         (AC, ac_ev.get(i - 2, ac_prev_part))],
                                        kmm, inc=PE)
                        ac_ev[i] = step("scalar",
                                        [(PE, pe_mm[i]), (VE, ve_rp.get(i - 2, 0))],
                                        lambda q, sl=sl: q.activation(
                                            kvb[sl][:], psKV[sl][:, :], AF.Copy),
                                        inc=AC)

                        def krope(q, sl=sl, jj=jj, h=h):
                            return emit_rope(q, kvb[sl], jj % 2,
                                             K8[:, h * TKV + jj * BL:h * TKV + (jj + 1) * BL])
                        wk_ = [(AC, ac_ev[i]), (SC, cs)]
                        if grp > 0 and i == 0:
                            wk_.append((PE, pe_k8_free))
                        ve_rp[i] = step("vector", wk_, krope, inc=VE)
                        rope_prev = ve_rp[i]
                pe_wbuf_free = PE.n

                a1v_ps = ExitStack()
                psVT0 = a1v_ps.enter_context(nc.psum_tensor(f"psVT0_{grp}", [128, 256], BF16))
                psVT1 = a1v_ps.enter_context(nc.psum_tensor(f"psVT1_{grp}", [128, 256], BF16))
                psVT = [psVT0, psVT1]
                ve_part0 = VE.n
                ac_part0 = AC.n
                folds = stream_strips(wvT_d, nwT, grp)
                pe_mm2 = {}
                ac_ev2 = {}
                pe_tr = {}
                ve_vte = {}
                pe_prev_part = PE.n
                ac_prev_part = AC.n
                for jj in range(NKB):
                    ld_xn[jj] = step("sync",
                                     [(SC, sc_xn[jj]),
                                      (PE, pe_mm2.get((jj - 2) * HG + HG - 1,
                                                      pe_prev_part if jj < 2 else 0))],
                                     lambda q, jj=jj: q.dma_start(
                                         out=xnb[jj % 2][:].rearrange("p (a t) -> p a t", t=BL),
                                         in_=xn_scr[jj].rearrange("a p t -> p a t")),
                                     inc=LD, k=16)
                    for h in range(HG):
                        hh = grp * HG + h
                        i = jj * HG + h
                        sl = i % 2

                        def vmm(q, jj=jj, h=h, hh=hh, sl=sl):
                            for a in range(DT):
                                q.matmul(psKV[sl][:, :],
                                         wbuf[:, h * D + a * 128:h * D + (a + 1) * 128],
                                         xnb[jj % 2][:, a * BL:(a + 1) * BL],
                                         start=(a == 0), stop=False)
                            return q.matmul(psKV[sl][:, :],
                                            vB_b[0:16, hh * 128:(hh + 1) * 128],
                                            u_sb[0:16, 2 * TKV + jj * BL:2 * TKV + (jj + 1) * BL],
                                            start=False, stop=True)
                        pe_mm2[i] = step("tensor",
                                         [(LD, ld_xn[jj]), (VE, folds[h]), (VE, ve_lorab),
                                          (AC, ac_ev2.get(i - 2, ac_prev_part))],
                                         vmm, inc=PE)
                        ac_ev2[i] = step("scalar",
                                         [(PE, pe_mm2[i]), (PE, pe_tr.get(i - 2, 0))],
                                         lambda q, sl=sl: q.activation(
                                             kvb[sl][:], psKV[sl][:, :], AF.Copy),
                                         inc=AC)

                        def vtr(q, sl=sl):
                            last = None
                            for qt in range(4):
                                last = q.transpose(
                                    psVT[qt % 2][:, (qt // 2) * 128:(qt // 2 + 1) * 128],
                                    kvb[sl][:, qt * 128:(qt + 1) * 128], idb[:])
                            return last
                        wv_ = [(AC, ac_ev2[i]), (VE, ve_vte.get(i - 1, 0))]
                        if i == 0:
                            wv_ += [(VE, ve_part0), (AC, ac_part0)]
                        pe_tr[i] = step("tensor", wv_, vtr, inc=PE)

                        def vte(q, jj=jj, h=h):
                            base = h * TKV + jj * 4 * 128
                            q.tensor_copy(Vk8[:, base:base + 128], psVT0[:, 0:128])
                            q.tensor_copy(Vk8[:, base + 256:base + 384], psVT0[:, 128:256])
                            q.tensor_copy(Vk8[:, base + 128:base + 256], psVT1[:, 0:128])
                            return q.tensor_copy(Vk8[:, base + 384:base + 512],
                                                 psVT1[:, 128:256])
                        wv2_ = [(PE, pe_tr[i])]
                        if grp > 0 and i == 0:
                            wv2_.append((PE, pe_k8_free))
                        ve_vte[i] = step("vector", wv2_, vte, inc=VE)
                pe_wbuf_free = PE.n
                a1v_ps.close()

                if phases >= 3:
                    a2_ps = ExitStack()
                    psSA = a2_ps.enter_context(nc.psum_tensor(f"psSA_{grp}", [128, 1024], F32))
                    psSB = a2_ps.enter_context(nc.psum_tensor(f"psSB_{grp}", [128, 1024], F32))
                    psPTa = a2_ps.enter_context(nc.psum_tensor(f"psPTa_{grp}", [128, 384], F32))
                    psPTb = a2_ps.enter_context(nc.psum_tensor(f"psPTb_{grp}", [128, 256], F32))
                    psS = [psSA, psSB]
                    psAVO = psKV1
                    ve_part0 = VE.n
                    ac_part0 = AC.n
                    folds = stream_strips(wqT_d, nwTq, grp)
                    ac_prev_part = AC.n
                    un = 0
                    pe_sc = {}
                    ve_mx = {}
                    ac_ex = {}
                    ve_sm = {}
                    gp_dv = {}
                    pe_t2 = {}
                    ac_pt = {}
                    pe_av = {}
                    ac_ao = {}
                    sc_at = {}
                    ve_qr = {}
                    ac_qe = {}
                    pe_prev_part = PE.n
                    for j in range(NB):
                        jj = j + 1
                        ldq = step("sync",
                                   [(SC, sc_xn[jj]),
                                    (PE, pe_sc.get((j - 2) * HG * 4 + HG * 4 - 1,
                                                   pe_prev_part if j < 2 else 0))],
                                   lambda q, jj=jj: q.dma_start(
                                       out=xnb[jj % 2][:].rearrange("p (a t) -> p a t", t=BL),
                                       in_=xn_scr[jj].rearrange("a p t -> p a t")),
                                   inc=LD, k=16)
                        cs = load_costab(jj, rope_prev)
                        for h in range(HG):
                            hh = grp * HG + h
                            ih = j * HG + h
                            hsl = ih % 2

                            def qmm(q, jj=jj, h=h, hh=hh):
                                for a in range(DT):
                                    q.matmul(psKV[0][:, :],
                                             wbuf[:, h * D + a * 128:h * D + (a + 1) * 128],
                                             xnb[jj % 2][:, a * BL:(a + 1) * BL],
                                             start=(a == 0), stop=False)
                                return q.matmul(psKV[0][:, :],
                                                qB_b[0:16, hh * 128:(hh + 1) * 128],
                                                u_sb[0:16, jj * BL:(jj + 1) * BL],
                                                start=False, stop=True)
                            pe_q = step("tensor",
                                        [(LD, ldq), (VE, folds[h]), (VE, ve_lorab),
                                         (AC, ac_qe.get(ih - 1, ac_prev_part))],
                                        qmm, inc=PE)
                            ac_qe[ih] = step("scalar",
                                             [(PE, pe_q), (VE, ve_qr.get(ih - 2, 0))],
                                             lambda q, hsl=hsl: q.activation(
                                                 kvb[hsl][:], psKV[0][:, :], AF.Copy),
                                             inc=AC)

                            def qrope(q, hsl=hsl, jj=jj):
                                return emit_rope(q, kvb[hsl], jj % 2, Qh[hsl][:])
                            ve_qr[ih] = step("vector",
                                             [(AC, ac_qe[ih]), (SC, cs),
                                              (PE, pe_sc.get(un - 5, 0))],
                                             qrope, inc=VE)
                            rope_prev = ve_qr[ih]

                            for qt in range(4):
                                u = un
                                un += 1
                                usl = u % 2
                                kb = j * 512 + qt * 128
                                mi = qt if j == 0 else 4

                                def smm(q, h=h, hsl=hsl, qt=qt, kb=kb, mi=mi, usl=usl):
                                    S = psS[usl]
                                    q.matmul(S[:, 0:512],
                                             Qh[hsl][:, qt * 128:(qt + 1) * 128],
                                             K8[:, h * TKV + kb:h * TKV + kb + 512],
                                             start=True, stop=False)
                                    q.matmul(S[:, 512:640],
                                             Qh[hsl][:, qt * 128:(qt + 1) * 128],
                                             K8[:, h * TKV + kb + 512:h * TKV + kb + 640],
                                             start=True, stop=False)
                                    q.matmul(S[:, 0:512], idb[:],
                                             masks_s[:, mi * 640:mi * 640 + 512],
                                             start=False, stop=False)
                                    return q.matmul(S[:, 512:640], idb[:],
                                                    masks_s[:, mi * 640 + 512:(mi + 1) * 640],
                                                    start=False, stop=True)
                                w_s = [(VE, ve_qr[ih]), (LD, ld_m),
                                       (AC, ac_ex.get(u - 2, 0)),
                                       (VE, ve_mx.get(u - 2, 0))]
                                if u == 0:
                                    w_s += [(VE, ve_part0), (AC, ac_part0)]
                                pe_sc[u] = step("tensor", w_s, smm, inc=PE)
                                ve_mx[u] = step("vector",
                                                [(PE, pe_sc[u]), (AC, ac_ex.get(u - 4, 0))],
                                                lambda q, usl=usl, u=u: q.reduce_max(
                                                    statm[:, u % 4:u % 4 + 1],
                                                    psS[usl][:, 0:640], axis=AX.X,
                                                    negate=True),
                                                inc=VE)
                                ac_ex[u] = step("scalar",
                                                [(VE, ve_mx[u]), (PE, pe_t2.get(u - 2, 0))],
                                                lambda q, usl=usl, u=u: q.activation(
                                                    Pb[usl][:], psS[usl][:, 0:640], AF.Exp,
                                                    bias=statm[:, u % 4:u % 4 + 1]),
                                                inc=AC)
                                ve_sm[u] = step("vector",
                                                [(AC, ac_ex[u]), (VE, gp_dv.get(u - 4, 0))],
                                                lambda q, usl=usl, u=u: q.reduce_sum(
                                                    statm[:, 4 + u % 4:5 + u % 4],
                                                    Pb[usl][:], axis=AX.X),
                                                inc=VE)
                                ve_rc = step("vector",
                                             [(VE, ve_sm[u])],
                                             lambda q, u=u: q.reciprocal(
                                                 statm[:, 8 + u % 4:9 + u % 4],
                                                 statm[:, 4 + u % 4:5 + u % 4]),
                                             inc=VE)
                                gp_dv[u] = step("vector",
                                                [(VE, ve_rc), (PE, pe_t2.get(u - 2, 0))],
                                                lambda q, usl=usl, u=u: q.tensor_scalar_mul(
                                                    dts[usl][:], idb[:],
                                                    statm[:, 8 + u % 4:9 + u % 4]),
                                                inc=VE)

                                def ptr(q, usl=usl):
                                    last = None
                                    for c in range(5):
                                        dst = psPTa if c % 2 == 0 else psPTb
                                        last = q.matmul(
                                            dst[:, (c // 2) * 128:(c // 2 + 1) * 128],
                                            Pb[usl][:, c * 128:(c + 1) * 128],
                                            dts[usl][:], start=True, stop=True)
                                    return last
                                pe_t2[u] = step("tensor",
                                                [(VE, gp_dv[u]), (AC, ac_pt.get(u - 1, 0))],
                                                ptr, inc=PE)

                                def pte(q, usl=usl):
                                    pt3 = PTs[usl][:].rearrange("p (c x) -> p c x", x=128)
                                    q.activation(pt3[:, 0:5:2, :],
                                                 psPTa[:].rearrange("p (c x) -> p c x", x=128),
                                                 AF.Copy)
                                    return q.activation(
                                        pt3[:, 1:5:2, :],
                                        psPTb[:].rearrange("p (c x) -> p c x", x=128),
                                        AF.Copy)
                                ac_pt[u] = step("scalar",
                                                [(PE, pe_t2[u]), (PE, pe_av.get(u - 2, 0))],
                                                pte, inc=AC)

                                def avm(q, h=h, j=j, qt=qt, usl=usl):
                                    last = None
                                    for i5 in range(5):
                                        kt = j * 4 + qt + i5
                                        last = q.matmul(
                                            psAVO[:, qt * 128:(qt + 1) * 128],
                                            Vk8[:, h * TKV + kt * 128:h * TKV + (kt + 1) * 128],
                                            PTs[usl][:, i5 * 128:(i5 + 1) * 128],
                                            start=(i5 == 0), stop=(i5 == 4))
                                    return last
                                pe_av[u] = step("tensor",
                                                [(AC, ac_pt[u]), (AC, ac_ao.get(ih - 1, 0))],
                                                avm, inc=PE)

                            ac_ao[ih] = step("scalar",
                                             [(PE, pe_av[un - 1]), (SC, sc_at.get(ih - 2, 0))],
                                             lambda q, ih=ih: q.activation(
                                                 ath[ih % 2][:], psAVO[:, :], AF.Copy),
                                             inc=AC)
                            sc_at[ih] = step("scalar", [(AC, ac_ao[ih])],
                                             lambda q, j=j, hh=hh, ih=ih: q.dma_start(
                                                 out=attn_scr[j, hh], in_=ath[ih % 2][:]),
                                             inc=SC, k=16)
                    pe_k8_free = PE.n
                    pe_wbuf_free = PE.n
                    a2_ps.close()
            sc_attn_all = SC.n

        att_scope.close()
        barrier()

    pa_scope.close()

    sc_x1 = [0] * NB
    if phases >= 4:
        c_scope = ExitStack()
        _aec = c_scope.enter_context
        wobuf = _aec(nc.sbuf_tensor("wobuf", [128, DT * D], BF16))
        atin = _aec(nc.sbuf_tensor("atin", [128, NH * BL], BF16))
        xcb0 = _aec(nc.sbuf_tensor("xcb0", [128, DT * BL], F32))
        xcb1 = _aec(nc.sbuf_tensor("xcb1", [128, DT * BL], F32))
        oA_b = _aec(nc.sbuf_tensor("oA_b", [128, DT * 16], BF16))
        oB_b = _aec(nc.sbuf_tensor("oB_b", [16, D], BF16))
        stoa = _aec(nc.sbuf_tensor("stoa", [128, DT * 16], F32))
        stB2 = _aec(nc.sbuf_tensor("stB2", [16, D], F32))
        uo_sb = _aec(nc.sbuf_tensor("uo_sb", [16, BL], BF16))
        tmpc = _aec(nc.sbuf_tensor("tmpc", [128, BL], F32))
        rsl = _aec(nc.sbuf_tensor("rsl", [1, 2 * BL], F32))
        psO0 = _aec(nc.psum_tensor("psO0", [128, 512], F32))
        psO1 = _aec(nc.psum_tensor("psO1", [128, 512], F32))
        psUo = _aec(nc.psum_tensor("psUo", [16, 512], F32))
        psBcR = _aec(nc.psum_tensor("psBcR", [128, 512], F32))
        psBcM = _aec(nc.psum_tensor("psBcM", [128, 512], F32))
        if True:

            psO = [psO0, psO1]
            xcb = [xcb0, xcb1]

            ldo = step("sync", [], lambda q: q.dma_start(
                out=stoa[:].rearrange("p (a r) -> p a r", r=16),
                in_=oAT_d.rearrange("(a p) r -> p a r", p=128)), inc=LD, k=16)
            ldo2 = step("sync", [], lambda q: q.dma_start(out=stB2[:], in_=oBT_d[:]),
                        inc=LD, k=16)

            def ocast(q):
                q.tensor_copy(oA_b[:], stoa[:])
                return q.tensor_copy(oB_b[:], stB2[:])
            ve_oc = step("vector", [(LD, ldo2), (LD, ldo)], ocast, inc=VE)

            wo_folds = []
            for dt_ in range(DT):
                ldv = step("sync", [(VE, wo_folds[-1] if wo_folds else 0)],
                           lambda q, dt_=dt_: q.dma_start(
                               out=wstage[:].rearrange("p (a o) -> p a o", o=128),
                               in_=woT_d[:, dt_ * 128:(dt_ + 1) * 128].rearrange(
                                   "(a p) o -> p a o", p=128)),
                           inc=LD, k=16)

                def wof(q, dt_=dt_):
                    last = None
                    for a in range(DT):
                        last = q.tensor_copy(
                            wobuf[:, dt_ * D + a * 128:dt_ * D + (a + 1) * 128],
                            wstage[:, a * 128:(a + 1) * 128])
                    return last
                wo_folds.append(step("vector", [(LD, ldv)], wof, inc=VE))
            wo_ready = wo_folds[-1]

            pe_cblk = [0] * NB
            ve_res = [0] * NB
            pe_bcr_prev = 0
            for j in range(NB):
                ld_at = step("sync",
                             [(SC, sc_attn_all), (PE, pe_cblk[j - 1] if j >= 1 else 0)],
                             lambda q, j=j: q.dma_start(
                                 out=atin[:].rearrange("p (h t) -> p h t", t=BL),
                                 in_=attn_scr[j].rearrange("h p t -> p h t")),
                             inc=LD, k=16)
                ld_xc = step("sync", [(SC, sc_x1[j - 2] if j >= 2 else 0)],
                             lambda q, j=j: q.dma_start(
                                 out=xcb[j % 2][:].rearrange("p (a t) -> p a t", t=BL),
                                 in_=xe[:, :, (j + 1) * BL:(j + 2) * BL].rearrange(
                                     "a p t -> p a t")),
                             inc=LD, k=16)
                def rw_b(q, j=j):
                    q.dma_start(out=rsl[0:1, 0:BL],
                                in_=row_scr[0, j + 1].unsqueeze(0)).then_inc(LD.h, 16)
                    return q.dma_start(out=rsl[0:1, BL:2 * BL],
                                       in_=row_scr[1, j + 1].unsqueeze(0))
                LD.n += 16
                ld_rw = step("sync",
                             [(SC, sc_row), (PE, pe_bcr_prev)],
                             rw_b, inc=LD, k=16)

                def uomm(q):
                    last = None
                    for a in range(DT):
                        last = q.matmul(psUo[:, :], oA_b[:, a * 16:(a + 1) * 16],
                                        atin[:, a * BL:(a + 1) * BL],
                                        start=(a == 0), stop=(a == DT - 1))
                    return last
                pe_uo = step("tensor",
                             [(LD, ld_at), (VE, ve_oc),
                              (VE, ve_res[j - 1] if j >= 1 else 0)],
                             uomm, inc=PE)
                ve_uo = step("vector", [(PE, pe_uo)],
                             lambda q: q.tensor_scalar_mul(uo_sb[:], psUo[:, :], LORA_SC),
                             inc=VE)

                def bcr(q):
                    q.matmul(psBcR[:, :], onerf[:], rsl[0:1, 0:BL], start=True, stop=True)
                    return q.matmul(psBcM[:, :], onerf[:], rsl[0:1, BL:2 * BL],
                                    start=True, stop=True)
                pe_bcr = step("tensor",
                              [(LD, ld_rw), (VE, ve_res[j - 1] if j >= 1 else 0)],
                              bcr, inc=PE)
                pe_bcr_prev = pe_bcr

                ve_dt = {}
                for dt_ in range(DT):
                    def omm(q, dt_=dt_):
                        for a in range(DT):
                            q.matmul(psO[dt_ % 2][:, :],
                                     wobuf[:, dt_ * D + a * 128:dt_ * D + (a + 1) * 128],
                                     atin[:, a * BL:(a + 1) * BL],
                                     start=(a == 0), stop=False)
                        return q.matmul(psO[dt_ % 2][:, :],
                                        oB_b[0:16, dt_ * 128:(dt_ + 1) * 128],
                                        uo_sb[:], start=False, stop=True)
                    pe_o = step("tensor",
                                [(VE, ve_uo),
                                 (VE, wo_ready if (j == 0 and dt_ == 0) else 0),
                                 (VE, ve_dt.get(dt_ - 2, 0))],
                                omm, inc=PE)

                    def res(q, dt_=dt_, j=j):
                        xs = xcb[j % 2][:, dt_ * BL:(dt_ + 1) * BL]
                        q.tensor_mul(tmpc[:], xs, psBcR[:, :])
                        q.tensor_add(tmpc[:], tmpc[:], psBcM[:, :])
                        return q.tensor_add(xs, tmpc[:], psO[dt_ % 2][:, :])
                    ve_dt[dt_] = step("vector",
                                      [(PE, pe_o), (PE, pe_bcr), (LD, ld_xc)],
                                      res, inc=VE)
                ve_res[j] = ve_dt[DT - 1]
                pe_cblk[j] = PE.n
                sc_x1[j] = step("scalar", [(VE, ve_res[j])],
                                lambda q, j=j: q.dma_start(
                                    out=x1_scr[j].rearrange("a p t -> p a t"),
                                    in_=xcb[j % 2][:].rearrange("p (a t) -> p a t", t=BL)),
                                inc=SC, k=16)

        c_scope.close()
        barrier()

    st_y = [0, 0]
    if phases >= 5:
        ffn_scope = ExitStack()
        _aec = ffn_scope.enter_context
        ybuf = _aec(nc.sbuf_tensor("ybuf", [128, DT * 1024], F32))
        xn2 = _aec(nc.sbuf_tensor("xn2", [128, DT * 1024], BF16))
        Hbuf = _aec(nc.sbuf_tensor("Hbuf", [128, NIT * 1024], BF16))
        w1s0 = _aec(nc.sbuf_tensor("w1s0", [128, DT * 128], BF16))
        w1s1 = _aec(nc.sbuf_tensor("w1s1", [128, DT * 128], BF16))
        w3s0 = _aec(nc.sbuf_tensor("w3s0", [128, DT * 128], BF16))
        w3s1 = _aec(nc.sbuf_tensor("w3s1", [128, DT * 128], BF16))
        w2s0 = _aec(nc.sbuf_tensor("w2s0", [128, NIT * 512], BF16))
        w2s1 = _aec(nc.sbuf_tensor("w2s1", [128, NIT * 512], BF16))
        selm_s = _aec(nc.sbuf_tensor("selm_s", [16, 16 * 128], BF16))
        fc0 = _aec(nc.sbuf_tensor("fcst0", [128, BL], BF16))
        fc1 = _aec(nc.sbuf_tensor("fcst1", [128, BL], BF16))
        fs0 = _aec(nc.sbuf_tensor("fsq0", [128, BL], BF16))
        fs1 = _aec(nc.sbuf_tensor("fsq1", [128, BL], BF16))
        sA0 = _aec(nc.sbuf_tensor("sA0", [128, BL], F32))
        sA1 = _aec(nc.sbuf_tensor("sA1", [128, BL], F32))
        T1b = _aec(nc.sbuf_tensor("T1b", [128, BL], F32))
        ftmp = _aec(nc.sbuf_tensor("ftmp", [128, BL], F32))
        zr_sb = _aec(nc.sbuf_tensor("zr_sb", [16, BL], F32))
        z_sb = _aec(nc.sbuf_tensor("z_sb", [16, BL], F32))
        zt_sb = _aec(nc.sbuf_tensor("zt_sb", [128, 64], F32))
        m1_sb = _aec(nc.sbuf_tensor("m1_sb", [128, 64], F32))
        m2_sb = _aec(nc.sbuf_tensor("m2_sb", [128, 64], F32))
        v12 = _aec(nc.sbuf_tensor("v12", [128, 16], F32))
        dwT_sb = _aec(nc.sbuf_tensor("dwT_sb", [16, 1024], BF16))
        psA0 = _aec(nc.psum_tensor("psA0", [128, 512], F32))
        psA1f = _aec(nc.psum_tensor("psA1f", [128, 512], F32))
        psB0 = _aec(nc.psum_tensor("psB0", [128, 512], F32))
        psB1f = _aec(nc.psum_tensor("psB1f", [128, 512], F32))
        psY0 = _aec(nc.psum_tensor("psY0", [128, 512], F32))
        psY1f = _aec(nc.psum_tensor("psY1f", [128, 512], F32))
        psW0 = _aec(nc.psum_tensor("psW0", [128, 512], F32))
        psW1 = _aec(nc.psum_tensor("psW1", [128, 512], F32))
        if True:

            psA = [psA0, psA1f]
            psB = [psB0, psB1f]
            psY = [psY0, psY1f]
            psW = [psW0, psW1]
            w1s = [w1s0, w1s1]
            w3s = [w3s0, w3s1]
            w2s = [w2s0, w2s1]
            fcs = [fc0, fc1]
            fss = [fs0, fs1]
            sAs = [sA0, sA1]

            ld_sel = step("sync", [], lambda q: q.dma_start(out=selm_s[:], in_=selm_d[:]),
                          inc=LD, k=16)

            ve_t1_last = 0
            for hf in range(2):
                ld_y = 0
                for b2 in range(2):
                    ld_y = step("sync",
                                [(SC, sc_x1[2 * hf + b2]),
                                 (SC, st_y[0] if hf == 1 else 0)],
                                lambda q, hf=hf, b2=b2: q.dma_start(
                                    out=ybuf[:].rearrange("p (a t) -> p a t", t=1024)[
                                        :, :, b2 * 512:(b2 + 1) * 512],
                                    in_=x1_scr[2 * hf + b2].rearrange("a p t -> p a t")),
                                inc=LD, k=16)

                ve_xn2_ts = [0, 0]
                row_dwe_ts = [0, 0]
                for ts in range(2):
                    pe_tile = {}
                    for a in range(DT):
                        sl = a % 2
                        w_ = [(LD, ld_y)]
                        if a >= 2:
                            w_.append((PE, pe_tile[a - 2]))
                        if a == 0 and (ts == 1 or hf == 1):
                            w_.append((VE, ve_xn2_ts[ts - 1] if ts == 1 else VE.n))

                        def fcast(q, a=a, sl=sl, ts=ts):
                            ys = ybuf[:, a * 1024 + ts * 512:a * 1024 + (ts + 1) * 512]
                            q.activation(fcs[sl][:], ys, AF.Copy)
                            return q.activation(fss[sl][:], ys, AF.Square)
                        ac_v = step("scalar", w_, fcast, inc=AC)

                        def fstat(q, a=a, sl=sl):
                            q.matmul(psY0[0:1, :], onecb[:], fcs[sl][:],
                                     start=(a == 0), stop=(a == DT - 1))
                            return q.matmul(psY1f[0:1, :], onecb[:], fss[sl][:],
                                            start=(a == 0), stop=(a == DT - 1))
                        pe_tile[a] = step("tensor", [(AC, ac_v)], fstat, inc=PE)
                    pe_fstat = pe_tile[DT - 1]

                    def rstep(eng, build, extra=None):
                        w_ = [(RW, RW.n)]
                        if extra:
                            w_.append(extra)
                        return step(eng, w_, build, inc=RW)

                    rstep("vector", lambda q: q.tensor_scalar_mul(
                        rows[0:1, RM:RM + BL], psY0[0:1, :], 1.0 / D),
                        extra=(PE, pe_fstat))
                    rstep("vector", lambda q: q.tensor_scalar_mul(
                        rows[0:1, RQ:RQ + BL], psY1f[0:1, :], 1.0 / D))
                    rstep("vector", lambda q: q.tensor_mul(
                        rows[0:1, RT:RT + BL], rows[0:1, RM:RM + BL], rows[0:1, RM:RM + BL]))
                    rstep("vector", lambda q: q.tensor_sub(
                        rows[0:1, RV:RV + BL], rows[0:1, RQ:RQ + BL], rows[0:1, RT:RT + BL]))
                    rstep("vector", lambda q: q.tensor_scalar_add(
                        rows[0:1, RT:RT + BL], rows[0:1, RV:RV + BL], EPS_LN))
                    rstep("scalar", lambda q: q.activation(
                        rows[0:1, RT:RT + BL], rows[0:1, RT:RT + BL], AF.Sqrt))
                    rstep("vector", lambda q: q.reciprocal(
                        rows[0:1, RC:RC + BL], rows[0:1, RT:RT + BL]))
                    rstep("vector", lambda q: q.tensor_mul(
                        rows[0:1, RE:RE + BL], rows[0:1, RM:RM + BL], rows[0:1, RC:RC + BL]))
                    row_c2 = rstep("vector", lambda q: q.tensor_scalar_mul(
                        rows[0:1, RE:RE + BL], rows[0:1, RE:RE + BL], -1.0))

                    def zmm(q, ts=ts):
                        last = None
                        for a in range(DT):
                            last = q.matmul(
                                psW0[0:16, :], gw_s[:, a * 16:(a + 1) * 16],
                                ybuf[:, a * 1024 + ts * 512:a * 1024 + (ts + 1) * 512],
                                start=(a == 0), stop=(a == DT - 1))
                        return last
                    pe_z = step("tensor", [(LD, ld_y), (VE, ve_t1_last)], zmm, inc=PE)
                    ve_zr = step("vector", [(PE, pe_z)],
                                 lambda q: q.tensor_copy(zr_sb[:], psW0[0:16, :]), inc=VE)

                    def zbc(q):
                        q.matmul(psW0[:, :], onerf[:], rows[0:1, RC:RC + BL],
                                 start=True, stop=True)
                        return q.matmul(psW1[0:16, :], Se_row[:],
                                        rows[0:1, RE:RE + BL], start=True, stop=True)
                    pe_zbc = step("tensor",
                                  [(RW, row_c2), (VE, ve_zr), (SC, p0_se_done),
                                   (VE, ve_t1_last)],
                                  zbc, inc=PE)

                    rstep("vector", lambda q: q.tensor_mul(
                        z_sb[:], zr_sb[:], psW0[0:16, :]), extra=(PE, pe_zbc))
                    rstep("vector", lambda q: q.tensor_add(
                        z_sb[:], z_sb[:], psW1[0:16, :]))
                    row_zs = rstep("scalar", lambda q: q.activation(
                        z_sb[:], z_sb[:], AF.Sigmoid))

                    def ztr(q):
                        last = None
                        for c in range(4):
                            last = q.matmul(psA0[:, c * 16:(c + 1) * 16],
                                            z_sb[0:16, c * 128:(c + 1) * 128],
                                            id16f[:], start=True, stop=True)
                        return last
                    pe_zt = step("tensor", [(RW, row_zs), (VE, ve_t1_last)], ztr, inc=PE)
                    rstep("vector", lambda q: q.tensor_copy(
                        zt_sb[:], psA0[:, 0:64]), extra=(PE, pe_zt))

                    zt3 = zt_sb[:].rearrange("p (c e) -> p c e", e=16)
                    m13 = m1_sb[:].rearrange("p (c e) -> p c e", e=16)
                    m23 = m2_sb[:].rearrange("p (c e) -> p c e", e=16)
                    rstep("vector", lambda q: q.reduce_max(v12[:, 0:4], zt3, axis=AX.X))
                    rstep("vector", lambda q: q.tensor_tensor(
                        m13, zt3,
                        v12[:, 0:4].rearrange("p c -> p c ()").broadcast_to([128, 4, 16]),
                        ALU.is_equal))
                    rstep("vector", lambda q: q.tensor_scalar_mul(
                        m2_sb[:], m1_sb[:], NEG_BIG))
                    rstep("vector", lambda q: q.tensor_add(m2_sb[:], m2_sb[:], zt_sb[:]))
                    rstep("vector", lambda q: q.reduce_max(v12[:, 4:8], m23, axis=AX.X))
                    rstep("vector", lambda q: q.tensor_tensor(
                        m23, m23,
                        v12[:, 4:8].rearrange("p c -> p c ()").broadcast_to([128, 4, 16]),
                        ALU.is_equal))
                    rstep("vector", lambda q: q.tensor_add(
                        v12[:, 8:12], v12[:, 0:4], v12[:, 4:8]))
                    rstep("vector", lambda q: q.reciprocal(
                        v12[:, 12:16], v12[:, 8:12]))
                    rstep("vector", lambda q: q.tensor_mul(
                        v12[:, 0:4], v12[:, 0:4], v12[:, 12:16]))
                    rstep("vector", lambda q: q.tensor_mul(
                        v12[:, 4:8], v12[:, 4:8], v12[:, 12:16]))
                    rstep("vector", lambda q: q.tensor_tensor(
                        m13, m13,
                        v12[:, 0:4].rearrange("p c -> p c ()").broadcast_to([128, 4, 16]),
                        ALU.mult))
                    rstep("vector", lambda q: q.tensor_tensor(
                        m23, m23,
                        v12[:, 4:8].rearrange("p c -> p c ()").broadcast_to([128, 4, 16]),
                        ALU.mult))
                    row_dw = rstep("vector", lambda q: q.tensor_add(
                        m1_sb[:], m1_sb[:], m2_sb[:]))

                    def dwtr(q):
                        last = None
                        for c in range(4):
                            last = q.matmul(psB0[0:16, c * 128:(c + 1) * 128],
                                            m1_sb[:, c * 16:(c + 1) * 16], idf[:],
                                            start=True, stop=True)
                        return last
                    pe_dw = step("tensor", [(RW, row_dw), (VE, ve_t1_last)], dwtr, inc=PE)
                    row_dwe_ts[ts] = rstep("vector", lambda q, ts=ts: q.tensor_copy(
                        dwT_sb[0:16, ts * 512:(ts + 1) * 512], psB0[0:16, 0:512]),
                        extra=(PE, pe_dw))

                    pe_e2 = step("tensor", [(RW, RW.n), (VE, ve_t1_last)],
                                 lambda q: q.matmul(psY0[:, :], onerf[:],
                                                    rows[0:1, RE:RE + BL],
                                                    start=True, stop=True),
                                 inc=PE)

                    def xn2_b(q, ts=ts):
                        last = None
                        for a in range(DT):
                            ys = ybuf[:, a * 1024 + ts * 512:a * 1024 + (ts + 1) * 512]
                            q.tensor_mul(ftmp[:], ys, psW0[:, :])
                            last = q.tensor_add(
                                xn2[:, a * 1024 + ts * 512:a * 1024 + (ts + 1) * 512],
                                ftmp[:], psY0[:, :])
                        return last
                    ve_xn2_ts[ts] = step("vector", [(PE, pe_e2)], xn2_b, inc=VE)

                ve_xn2 = ve_xn2_ts[1]

                pe_ab_ch = {}
                ve_h_u = {}
                ac_si_u = {}
                pe_y_dt = {}
                ve_ye = {}
                ld_w13_ch = {}
                ld_w2_ch = {}
                pe_webc = {}
                unf = 0
                pe_y_done_e = 0
                ve_h_ts_last = {0: 0, 1: 0}
                yk = 0

                for e in range(NEXP):
                    shared = e < 2
                    if shared:
                        w1src = sw1T_d[:, e * FI:(e + 1) * FI]
                        w3src = sw3T_d[:, e * FI:(e + 1) * FI]
                        w2src = sw2T_d[e * FI:(e + 1) * FI, :]
                    else:
                        w1src = w1T_d[e - 2]
                        w3src = w3T_d[e - 2]
                        w2src = w2T_d[e - 2]

                    if not shared:
                        def webc(q, e=e):
                            q.matmul(psW0[:, :], selm_s[0:16, (e - 2) * 128:(e - 1) * 128],
                                     dwT_sb[0:16, 0:512], start=True, stop=True)
                            return q.matmul(psW1[:, :],
                                            selm_s[0:16, (e - 2) * 128:(e - 1) * 128],
                                            dwT_sb[0:16, 512:1024], start=True, stop=True)
                        pe_webc[e] = step("tensor",
                                          [(RW, row_dwe_ts[1]), (VE, ve_t1_last),
                                           (VE, ve_xn2)],
                                          webc, inc=PE)

                    e_first_u = unf
                    for it in range(NIT):
                        csl = it % 2

                        def w13ld(q, it=it, csl=csl, w1src=w1src, w3src=w3src):
                            q.dma_start(
                                out=w1s[csl][:].rearrange("p (a i) -> p a i", i=128),
                                in_=w1src[:, it * 128:(it + 1) * 128].rearrange(
                                    "(a p) i -> p a i", p=128)).then_inc(LD.h, 16)
                            return q.dma_start(
                                out=w3s[csl][:].rearrange("p (a i) -> p a i", i=128),
                                in_=w3src[:, it * 128:(it + 1) * 128].rearrange(
                                    "(a p) i -> p a i", p=128))
                        LD.n += 16
                        if it >= 2:
                            wch = pe_ab_ch[(e, it - 2)]
                        elif e >= 1:
                            wch = pe_ab_ch[(e - 1, NIT - 2 + it)]
                        else:
                            wch = 0
                        ld_w13_ch[(e, it)] = step("sync", [(PE, wch)], w13ld, inc=LD, k=16)
                        for ts in range(2):
                            u = unf
                            unf += 1
                            psl = u % 2

                            def abmm(q, it=it, csl=csl, ts=ts, psl=psl):
                                last = None
                                for a in range(DT):
                                    q.matmul(psA[psl][:, :],
                                             w1s[csl][:, a * 128:(a + 1) * 128],
                                             xn2[:, a * 1024 + ts * 512:a * 1024 + (ts + 1) * 512],
                                             start=(a == 0), stop=(a == DT - 1))
                                for a in range(DT):
                                    last = q.matmul(psB[psl][:, :],
                                             w3s[csl][:, a * 128:(a + 1) * 128],
                                             xn2[:, a * 1024 + ts * 512:a * 1024 + (ts + 1) * 512],
                                             start=(a == 0), stop=(a == DT - 1))
                                return last
                            w_ = [(LD, ld_w13_ch[(e, it)]), (VE, ve_xn2),
                                  (AC, ac_si_u.get(u - 2, 0)), (VE, ve_h_u.get(u - 2, 0))]
                            if not shared:
                                w_.append((PE, pe_webc[e]))
                            pe_ab = step("tensor", w_, abmm, inc=PE)
                            if ts == 1:
                                pe_ab_ch[(e, it)] = pe_ab

                            ac_si_u[u] = step("scalar",
                                              [(PE, pe_ab), (VE, ve_h_u.get(u - 2, 0))],
                                              lambda q, psl=psl: q.activation(
                                                  sAs[psl][:], psA[psl][:, :], AF.Silu),
                                              inc=AC)

                            def hmul(q, psl=psl, it=it, ts=ts, shared=shared):
                                hdst = Hbuf[:, it * 1024 + ts * 512:it * 1024 + (ts + 1) * 512]
                                if shared:
                                    return q.tensor_mul(hdst, sAs[psl][:], psB[psl][:, :])
                                q.tensor_mul(T1b[:], sAs[psl][:],
                                             psW[ts][:, :])
                                return q.tensor_mul(hdst, T1b[:], psB[psl][:, :])
                            ve_h_u[u] = step("vector",
                                             [(AC, ac_si_u[u]),
                                              (PE, pe_y_done_e if u == e_first_u else 0)],
                                             hmul, inc=VE)
                            if not shared:
                                ve_t1_last = ve_h_u[u]
                            ve_h_ts_last[ts] = ve_h_u[u]

                    for dq in range(4):
                        def w2ld(q, dq=dq, w2src=w2src):
                            return q.dma_start(
                                out=w2s[dq % 2][:].rearrange("p (i o) -> p i o", o=512),
                                in_=w2src[:, dq * 512:(dq + 1) * 512].rearrange(
                                    "(i p) o -> p i o", p=128))
                        if dq >= 2:
                            wch2 = pe_y_dt[(e, dq - 2, 1, 3)]
                        elif e >= 1:
                            wch2 = pe_y_dt[(e - 1, dq + 2, 1, 3)]
                        else:
                            wch2 = 0
                        ld_w2_ch[(e, dq)] = step("sync", [(PE, wch2)], w2ld, inc=LD, k=16)
                        for ts in range(2):
                            for dtl in range(4):
                                dt_ = dq * 4 + dtl
                                yi = yk
                                yk += 1
                                ysl = yi % 2

                                def ymm(q, dq=dq, ts=ts, dtl=dtl, ysl=ysl):
                                    last = None
                                    for it in range(NIT):
                                        last = q.matmul(
                                            psY[ysl][:, :],
                                            w2s[dq % 2][:, it * 512 + dtl * 128:
                                                        it * 512 + (dtl + 1) * 128],
                                            Hbuf[:, it * 1024 + ts * 512:
                                                 it * 1024 + (ts + 1) * 512],
                                            start=(it == 0), stop=(it == NIT - 1))
                                    return last
                                pe_y = step("tensor",
                                            [(LD, ld_w2_ch[(e, dq)]),
                                             (VE, ve_h_ts_last[ts]),
                                             (VE, ve_ye.get(yi - 2, 0))],
                                            ymm, inc=PE)
                                pe_y_dt[(e, dq, ts, dtl)] = pe_y

                                def yev(q, dt_=dt_, ts=ts, ysl=ysl):
                                    ys = ybuf[:, dt_ * 1024 + ts * 512:
                                              dt_ * 1024 + (ts + 1) * 512]
                                    return q.tensor_add(ys, ys, psY[ysl][:, :])
                                ve_ye[yi] = step("vector", [(PE, pe_y)], yev, inc=VE)
                    pe_y_done_e = PE.n

                ve_half_done = ve_ye[yk - 1]
                st_y[hf] = step("scalar", [(VE, ve_half_done)],
                                lambda q, hf=hf: q.dma_start(
                                    out=yT.rearrange("a p t -> p a t")[
                                        :, :, hf * 1024:(hf + 1) * 1024],
                                    in_=ybuf[:].rearrange("p (a t) -> p a t", t=1024)),
                                inc=SC, k=16)

        ffn_scope.close()

    step("sync", [(SC, SC.n), (LD, LD.n)], lambda q: q.nop())

    st.close()
    nc.finalize()
    return nc


_NC_CACHE = {}


def _build_cached(phases=PHASES, debug=False):
    key = (phases, debug)
    if key not in _NC_CACHE:
        _NC_CACHE[key] = build_program(phases, debug)
    return _NC_CACHE[key]


def make_host_consts():
    qp = np.arange(128)[:, None]
    c = np.arange(640)[None, :]
    band = (c >= qp + 1) & (c <= qp + 512)
    consts = {
        "idb": np.eye(128, dtype=np.float32).astype(bfdt),
        "idf": np.eye(128, dtype=np.float32),
        "id16f": np.eye(16, dtype=np.float32),
        "onerb": np.ones((1, 128), dtype=bfdt),
        "onerf": np.ones((1, 128), dtype=np.float32),
        "onecb": np.ones((128, 1), dtype=bfdt),
        "onecf": np.ones((128, 1), dtype=np.float32),
        "serow_in": np.zeros((1, 16), dtype=np.float32),
    }
    selm = np.zeros((16, 16 * 128), dtype=np.float32)
    for e in range(16):
        selm[e, e * 128:(e + 1) * 128] = 1.0
    consts["selm"] = selm.astype(bfdt)
    return consts, band


def make_core_inputs(x, h, band):
    own = x[h * 2048:(h + 1) * 2048]
    if h == 1:
        halo = x[2048 - 512:2048]
    else:
        halo = np.zeros((512, 2048), dtype=np.float32)
    xext = np.concatenate([halo, own], axis=0)
    xeT = np.ascontiguousarray(xext.T)
    xe = np.ascontiguousarray(xeT.reshape(DT, 128, TKV))

    masks = np.empty((5, 128, 640), dtype=np.float32)
    for qt in range(4):
        m = band.copy()
        if h == 0:
            cc = np.arange(640)[None, :]
            m = m & (qt * 128 + cc >= 512)
        masks[qt] = np.where(m, 0.0, NEG_BIG)
    masks[4] = np.where(band, 0.0, NEG_BIG)
    masks = masks.astype(bfdt)

    pos = np.arange(h * 2048 - 512, h * 2048 + 2048, dtype=np.float64)
    pos = np.maximum(pos, 0.0)
    half = 64
    theta = 1.0 / (10000.0 ** (np.arange(half, dtype=np.float64) / half))
    ang = theta[:, None] * pos[None, :]
    cos = np.concatenate([np.cos(ang), np.cos(ang)], axis=0)
    sin = np.concatenate([-np.sin(ang), np.sin(ang)], axis=0)
    return xe, masks, cos.astype(np.float32).astype(bfdt), sin.astype(np.float32).astype(bfdt)


def prep_in_maps(inputs):
    g = {k: np.asarray(v) for k, v in inputs.items()}
    x = np.ascontiguousarray(g["L0_attn_x"], dtype=np.float32)
    consts, band = make_host_consts()

    qA, kA, vA = g["L0_attn_qA"], g["L0_attn_kA"], g["L0_attn_vA"]
    A = np.zeros((96, D), dtype=np.float32)
    A[0:16], A[32:48], A[64:80] = qA, kA, vA

    shared = {
        "wqT": np.ascontiguousarray(g["L0_attn_wq"].T.astype(np.float32)),
        "wkT": np.ascontiguousarray(g["L0_attn_wk"].T.astype(np.float32)),
        "wvT": np.ascontiguousarray(g["L0_attn_wv"].T.astype(np.float32)),
        "woT": np.ascontiguousarray(g["L0_attn_wo"].T.astype(np.float32)),
        "qkvAT": np.ascontiguousarray(A.T),
        "qBT": np.ascontiguousarray(g["L0_attn_qB"].T.astype(np.float32)),
        "kBT": np.ascontiguousarray(g["L0_attn_kB"].T.astype(np.float32)),
        "vBT": np.ascontiguousarray(g["L0_attn_vB"].T.astype(np.float32)),
        "oAT": np.ascontiguousarray(g["L0_attn_oA"].T.astype(np.float32)),
        "oBT": np.ascontiguousarray(g["L0_attn_oB"].T.astype(np.float32)),
        "nwT": np.ascontiguousarray(
            g["L0_attn_norm_w"].astype(np.float32).reshape(DT, 128).T),
        "gwT": np.ascontiguousarray(g["L0_ffn_gate_w"].T.astype(np.float32)),
        "w1T": np.ascontiguousarray(g["L0_ffn_w1"].transpose(0, 2, 1)),
        "w3T": np.ascontiguousarray(g["L0_ffn_w3"].transpose(0, 2, 1)),
        "w2T": np.ascontiguousarray(g["L0_ffn_w2"].transpose(0, 2, 1)),
        "sw1T": np.ascontiguousarray(g["L0_ffn_sw1"].T),
        "sw3T": np.ascontiguousarray(g["L0_ffn_sw3"].T),
        "sw2T": np.ascontiguousarray(g["L0_ffn_sw2"].T),
    }
    shared.update(consts)

    in_maps = []
    for c in range(8):
        b, h = c // 2, c % 2
        xe, masks, cos, sin = make_core_inputs(x[b], h, band)
        m = dict(shared)
        m.update({"xe": xe, "masks": masks, "costab": cos, "sintab": sin})
        in_maps.append(m)
    return in_maps


def kernel(**inputs):
    in_maps = prep_in_maps(inputs)
    nc = _build_cached(PHASES, False)
    res = run_bass_kernel_spmd(nc, in_maps, core_ids=list(range(8)))
    out = np.empty((4, 4096, 2048), dtype=np.float32)
    for c in range(8):
        b, h = c // 2, c % 2
        yt = np.asarray(res.results[c]["yT"], dtype=np.float32)
        out[b, h * 2048:(h + 1) * 2048, :] = yt.reshape(D, TOK).T
    return out
